# version 01 (final)
# baseline (speedup 1.0000x reference)
import numpy as np
import ml_dtypes
import concourse.bass as bass
import concourse.mybir as mybir
from concourse.bass_utils import run_bass_kernel_spmd
from contextlib import ExitStack

F32 = mybir.dt.float32
BF16 = mybir.dt.bfloat16
AF = mybir.ActivationFunctionType
ALU = mybir.AluOpType

P = 128
T = 512
NT = 8
D = 2048
ND = D // P
INNER = 4096
NI = INNER // P
INTER = 8192
NU = INTER // P
LSEQ = NT * T
SW = 512
SPG = 2048 // SW
RSQ2048 = float(2048.0 ** -0.5)
RSQ8192 = float(8192.0 ** -0.5)


def _had(n):
    h = np.array([[1.0]], dtype=np.float32)
    while h.shape[0] < n:
        h = np.block([[h, h], [h, -h]])
    return h


def build_program(stage=6, taps=(), detect_races=False, dump_fold=False):
    nc = bass.Bass("TRN2", target_bir_lowering=False,
                   detect_race_conditions=detect_races)
    have = lambda s: stage >= s

    xT = nc.dram_tensor("xT", [D, LSEQ], F32, kind="ExternalInput").ap()
    WinT = nc.dram_tensor("WinT", [D, INNER], F32, kind="ExternalInput").ap()
    WgT = nc.dram_tensor("WgT", [D, INNER], F32, kind="ExternalInput").ap()
    WoT = nc.dram_tensor("WoT", [INNER, D], F32, kind="ExternalInput").ap()
    WupT = nc.dram_tensor("WupT", [D, INTER], F32, kind="ExternalInput").ap()
    WgfT = nc.dram_tensor("WgfT", [D, INTER], F32, kind="ExternalInput").ap()
    WdnT = nc.dram_tensor("WdnT", [INTER, D], F32, kind="ExternalInput").ap()
    convw_c = nc.dram_tensor("convw_c", [P, 4 * NI], F32, kind="ExternalInput").ap()
    convb_c = nc.dram_tensor("convb_c", [P, NI], F32, kind="ExternalInput").ap()
    alog_c = nc.dram_tensor("alog_c", [P, NI], F32, kind="ExternalInput").ap()
    b_c = nc.dram_tensor("b_c", [P, NI], F32, kind="ExternalInput").ap()
    c_c = nc.dram_tensor("c_c", [P, NI], F32, kind="ExternalInput").ap()
    d_c = nc.dram_tensor("d_c", [P, NI], F32, kind="ExternalInput").ap()
    dtw_c = nc.dram_tensor("dtw_c", [P, NI], F32, kind="ExternalInput").ap()
    dtb_c = nc.dram_tensor("dtb_c", [64, 1], F32, kind="ExternalInput").ap()
    eye2_d = nc.dram_tensor("eye2_d", [P, 64], F32, kind="ExternalInput").ap()
    h128_d = nc.dram_tensor("h128_d", [P, P], BF16, kind="ExternalInput").ap()
    h64_d = nc.dram_tensor("h64_d", [64, 64], BF16, kind="ExternalInput").ap()
    suug_c = nc.dram_tensor("suug_c", [P, ND], F32, kind="ExternalInput").ap()
    wsu_c = nc.dram_tensor("wsu_c", [P, NU], F32, kind="ExternalInput").ap()
    wsg_c = nc.dram_tensor("wsg_c", [P, NU], F32, kind="ExternalInput").ap()
    sudp_c = nc.dram_tensor("sudp_c", [P, NU], F32, kind="ExternalInput").ap()
    wsd_c = nc.dram_tensor("wsd_c", [P, ND], F32, kind="ExternalInput").ap()
    onesb_d = nc.dram_tensor("onesb_d", [P, 1], BF16, kind="ExternalInput").ap()
    onesrow_d = nc.dram_tensor("onesrow_d", [1, P], F32, kind="ExternalInput").ap()

    outT = nc.dram_tensor("outT", [D, LSEQ], F32, kind="ExternalOutput").ap()
    tapd = {}
    tapshape = {"X": [INNER, LSEQ], "dt128": [P, LSEQ], "y": [INNER, LSEQ],
                "x2n": [D, LSEQ], "prodt": [INTER, LSEQ], "xht": [D, LSEQ]}
    for tname in taps:
        dt_ = BF16 if tname in ("prodt", "xht") else F32
        tapd[tname] = nc.dram_tensor("tap_" + tname, tapshape[tname], dt_,
                                     kind="ExternalOutput").ap()
    NTAP = len(tapd)

    blkWin = nc.dram_tensor("blkWin", [NI * ND, P, P], BF16, kind="Internal").ap()
    blkWg = nc.dram_tensor("blkWg", [NI * ND, P, P], BF16, kind="Internal").ap()
    blkWo = nc.dram_tensor("blkWo", [2 * NI * 8, P, P], BF16, kind="Internal").ap()
    blkWup = nc.dram_tensor("blkWup", [NU * ND, P, P], BF16, kind="Internal").ap()
    blkWgf = nc.dram_tensor("blkWgf", [NU * ND, P, P], BF16, kind="Internal").ap()
    fold_kind = "ExternalOutput" if dump_fold else "Internal"
    blkWdn = nc.dram_tensor("blkWdn", [2 * NU * 8, P, P], BF16, kind=fold_kind).ap()
    foldV = nc.dram_tensor("foldV", [64, P, D], BF16, kind=fold_kind).ap()

    es = ExitStack()
    sbt = lambda name, shape, dt: es.enter_context(nc.sbuf_tensor(name, shape, dt))

    bigA = sbt("bigA", [P, ND * T], F32)
    areaX = sbt("areaX", [P, NI * T], F32)
    areaCB = sbt("areaCB", [P, NI * T // 2], F32)
    xbf = sbt("xbf", [P, ND * T], BF16)
    stag = sbt("stag", [P, 5 * 16 * P], BF16)
    prepf = sbt("prepf", [P, 2 * SW], F32)
    prepb = sbt("prepb", [P, 2 * SW], BF16)
    foldst = sbt("foldst", [P, 2 * 1024], BF16)
    folde = sbt("folde", [P, 1024], BF16)
    projbuf = sbt("projbuf", [P, 2 * (T + 3)], F32)
    xbstag = sbt("xbstag", [P, 2 * T], BF16)
    dt128 = sbt("dt128", [P, T], F32)
    sc_dk = sbt("sc_dk", [P, T], F32)
    sc_dr = sbt("sc_dr", [P, T], F32)
    sc_st = sbt("sc_st", [P, T], F32)
    gbuf = sbt("gbuf", [P, 2 * T], F32)
    statv = sbt("statv", [P, T], BF16)
    statq = sbt("statq", [P, T], BF16)
    rows = sbt("rows", [1, 3 * T], F32)
    cols = sbt("cols", [P, 8], F32)
    colA = sbt("colA", [P, NI], F32)
    colwu = sbt("colwu", [P, NU], F32)
    colwg = sbt("colwg", [P, NU], F32)
    colsu2 = sbt("colsu2", [P, NU], F32)
    colmisc = sbt("colmisc", [P, 4 * NI + 6 * NI + 2 * ND], F32)
    ldiag = sbt("ldiag", [P, 2 * 64], BF16)
    h128s = sbt("h128s", [P, P], BF16)
    h64s = sbt("h64s", [64, 64], BF16)
    onesbs = sbt("onesbs", [P, 1], BF16)
    onesrs = sbt("onesrs", [1, P], F32)
    dtbs = sbt("dtbs", [64, 1], F32)
    eye2s = sbt("eye2s", [P, 64], F32)
    margin = sbt("margin", [P, NI * 3], F32)
    carry = sbt("carry", [P, NI], F32)

    regB = areaX[:, 0:ND * T]
    regC = areaX[:, ND * T:NI * T]
    prod = areaX.bitcast(BF16)[:, 0:NU * T]
    comb = areaCB.bitcast(BF16)[:, 0:NI * T]
    outr = areaCB[:, 0:ND * T]
    xh = areaCB.bitcast(BF16)[:, NI * T // 2:NI * T]
    X = areaX

    def ch(apx, c, w=T):
        return apx[:, c * w:(c + 1) * w]

    pb = [es.enter_context(nc.psum_tensor(f"pb{i}", [P, T], F32)) for i in range(8)]

    semnames = [
        "s_w0", "s_w1", "s_w2", "s_w3", "s_w4",
        "s_xin", "s_out", "s_dup", "s_aux", "s_plE", "s_plO", "s_plBE", "s_plBO", "s_prepE", "s_prepO",
        "s_folds", "s_foldld", "s_fbA", "s_fbB",
        "s_mmA", "s_mmDt", "s_mmDtc", "s_mmG", "s_mmOc", "s_mmO", "s_mmS",
        "s_mmSc", "s_mmF", "s_mmU", "s_mmGF", "s_mmDa", "s_mmD", "s_mmfA", "s_mmfB",
        "s_veP", "s_veW", "s_veXc", "s_veCv", "s_veXb", "s_veDt", "s_veScn",
        "s_veCb", "s_veR", "s_veRb", "s_veV", "s_veLN", "s_veN", "s_veXs",
        "s_veXh", "s_vePr", "s_veR2", "s_veO", "s_veAux", "s_veTap",
        "s_seE", "s_seX", "s_seG", "s_seDk", "s_seSp", "s_seQ", "s_seF",
        "s_seGF", "s_seSqt", "s_seFe", "s_seFa", "s_seAx", "s_veLd", "s_veMs",
        "s_xin2", "s_pl2E", "s_pl2O", "s_prep2E", "s_prep2O",
    ]
    S = {n: es.enter_context(nc.semaphore(n)) for n in semnames}

    def par(base, n):
        return (base + ("E" if n % 2 == 0 else "O"), 16 * (n // 2 + 1))

    def par_all(base, k):
        out = []
        ne = (k + 1) // 2
        no = k // 2
        if ne:
            out.append((base + "E", 16 * ne))
        if no:
            out.append((base + "O", 16 * no))
        return out

    fam_by_stage = {1: ("win",), 2: ("win",), 3: ("win",),
                    4: ("win", "wg", "wo"),
                    5: ("win", "wg", "wo", "wup", "wgf"),
                    6: ("win", "wg", "wo", "wup", "wgf", "wdn")}
    fams = fam_by_stage[min(stage, 6)]

    prep_fams = [f for f in ("win", "wg", "wo", "wup", "wgf") if f in fams]
    fam_slabs = {"win": NI * SPG, "wg": NI * SPG, "wo": 64 * 1024 // SW,
                 "wup": NU * SPG, "wgf": NU * SPG}
    prep_list = []
    for f in prep_fams:
        if f == "wo":
            for g in range(64):
                for sub in range(1024 // SW):
                    prep_list.append((f, g, sub))
        else:
            ng = {"win": NI, "wg": NI, "wup": NU, "wgf": NU}[f]
            for g in range(ng):
                for sub in range(SPG):
                    prep_list.append((f, g, sub))
    NPREP = len(prep_list)
    prep_cum = {}
    cnt = 0
    for f in prep_fams:
        prep_cum[f] = cnt
        cnt += fam_slabs[f]

    NWINGG = sum(fam_slabs[f] for f in prep_fams if f in ("win", "wg"))

    def prep_ready(fam, grp):
        if fam in ("win", "wg"):
            k = prep_cum[fam] + (grp + 1) * SPG
            return ("s_prep", k)
        if fam == "wo":
            k = prep_cum["wo"] - NWINGG + (grp + 1) * (1024 // SW)
            return ("s_prep2", k)
        k = prep_cum[fam] - NWINGG + (grp + 1) * SPG
        return ("s_prep2", k)

    def prep_src_dst(fam, g, sub):
        if fam in ("win", "wg", "wup", "wgf"):
            W = {"win": WinT, "wg": WgT, "wup": WupT, "wgf": WgfT}[fam]
            blk = {"win": blkWin, "wg": blkWg, "wup": blkWup, "wgf": blkWgf}[fam]
            m = g
            k0 = sub * (SW // P)
            nk = SW // P
            src = W.rearrange("(k p) i -> p k i", p=P)[:, k0:k0 + nk, m * P:(m + 1) * P]
            dst = blk[m * ND + k0: m * ND + k0 + nk].rearrange("n p q -> p n q")
            return src, dst
        if fam == "wo":
            h, c = g // NI, g % NI
            d0 = sub * (SW // P)
            nd_ = SW // P
            src = WoT[c * P:(c + 1) * P, h * 1024 + d0 * P: h * 1024 + (d0 + nd_) * P]
            src = src.rearrange("p (n q) -> p n q", n=nd_)
            dst = blkWo[(h * NI + c) * 8 + d0: (h * NI + c) * 8 + d0 + nd_]
            dst = dst.rearrange("n p q -> p n q")
            return src, dst
        raise KeyError(fam)

    def consumed_sem(tag):
        fam = tag[0]
        if fam == "win":
            return ("s_mmA", NI * tag[1] + tag[2] + 1)
        if fam == "wg":
            return ("s_mmG", NI * tag[1] + tag[2] + 1)
        if fam == "wo":
            _, j, h, cp = tag
            if cp == NI // 2 - 1:
                return ("s_mmO", 16 * j + 8 * h + 8)
            return ("s_mmOc", 60 * j + h * 30 + 2 * (cp + 1))
        if fam == "wup":
            return ("s_mmU", NU * tag[1] + tag[2] + 1)
        if fam == "wgf":
            return ("s_mmGF", NU * tag[1] + tag[2] + 1)
        if fam == "wdn":
            _, j, h, ap_ = tag
            if ap_ == NU // 2 - 1:
                return ("s_mmD", 16 * j + 8 * h + 8)
            return ("s_mmDa", 124 * j + h * 62 + 2 * (ap_ + 1))
        raise KeyError(fam)

    def stream_gate(tag):
        fam, j = tag[0], tag[1]
        if j > 0:
            return None
        if fam in ("win", "wg", "wup", "wgf"):
            return prep_ready(fam, tag[2])
        if fam == "wo":
            _, _, h, cp = tag
            return prep_ready("wo", h * NI + 2 * cp + 1)
        if fam == "wdn":
            _, _, h, ap_ = tag
            return ("s_fb", 16 * (2 * ap_ + 2))
        return None

    wplan = []
    slot_prev = [None] * 5

    def plan_push(tag, slot, blk_ap, blk0, nblk):
        prev = slot_prev[slot]
        slot_prev[slot] = tag
        wplan.append(dict(tag=tag, slot=slot, src=blk_ap, blk0=blk0, nblk=nblk,
                          prev=prev, gate=stream_gate(tag)))

    for j in range(NT):
        if "win" in fams:
            for m in range(NI):
                plan_push(("win", j, m), m % 3, blkWin, m * ND, ND)
        if "wg" in fams:
            for c in range(NI):
                plan_push(("wg", j, c), 3 + c % 2, blkWg, c * ND, ND)
        if "wo" in fams:
            for h in range(2):
                for cp in range(NI // 2):
                    plan_push(("wo", j, h, cp), cp % 3, blkWo, (h * NI + 2 * cp) * 8, 16)
        if "wup" in fams:
            for m in range(NU):
                plan_push(("wup", j, m), m % 3, blkWup, m * ND, ND)
                plan_push(("wgf", j, m), 3 + m % 2, blkWgf, m * ND, ND)
        if "wdn" in fams:
            for h in range(2):
                for ap_ in range(NU // 2):
                    plan_push(("wdn", j, h, ap_), ap_ % 3, blkWdn, (h * NU + 2 * ap_) * 8, 16)

    _slot_uses = [0] * 5
    w_wait = {}
    for e in wplan:
        _slot_uses[e["slot"]] += 1
        w_wait[e["tag"]] = (f"s_w{e['slot']}", 16 * _slot_uses[e["slot"]])
    w_slotof = {e["tag"]: e["slot"] for e in wplan}

    def stag_slot(slot, k=None):
        base = slot * 16 * P
        if k is None:
            return stag[:, base:base + 16 * P]
        return stag[:, base + k * P: base + (k + 1) * P]

    off = [0]
    def _col(n):
        apx = colmisc[:, off[0]:off[0] + n]
        off[0] += n
        return apx
    cw = [_col(NI) for _ in range(4)]
    cbcol = _col(NI)
    alograw = _col(NI)
    bcol = _col(NI)
    ccol = _col(NI)
    dcol = _col(NI)
    sucol = _col(ND)
    wsdcol = _col(ND)
    dtwcol = _col(NI)

    c_one = cols[:, 0:1]
    c_eps = cols[:, 1:2]
    c_i2048 = cols[:, 2:3]
    c_rsq2 = cols[:, 3:4]
    c_rsq8 = cols[:, 4:5]

    r_mean = rows[:, 0 * T:1 * T]
    r_var = rows[:, 1 * T:2 * T]
    r_rstd = rows[:, 2 * T:3 * T]

    aux_srcs = [(convw_c, colmisc[:, 0:4 * NI]),
                (convb_c, cbcol), (alog_c, alograw), (b_c, bcol), (c_c, ccol),
                (d_c, dcol), (suug_c, sucol), (wsd_c, wsdcol), (dtw_c, dtwcol),
                (dtb_c, dtbs[:]), (eye2_d, eye2s[:]), (h128_d, h128s[:]),
                (h64_d, h64s[:]), (wsu_c, colwu[:]), (wsg_c, colwg[:]),
                (sudp_c, colsu2[:]), (onesb_d, onesbs[:]), (onesrow_d, onesrs[:])]
    NAUX = len(aux_srcs)
    AUXV = 16 * NAUX

    NFA = 4 * P
    NFB = 64

    do_fold = have(6)
    n_wingg_g = min(sum(fam_slabs[f] for f in prep_fams if f in ("win", "wg")), NPREP)
    NSTT = 2 * ND if have(6) else ND
    NLN = 2 if have(6) else 1

    with es, nc.Block() as block:

        @block.sync
        def _(sync):
            def fold_loads_storesA():
                def fold_load(nn):
                    b, dsl = nn // 4, nn % 4
                    slot = nn % 2
                    if nn < 2:
                        sync.wait_ge(S["s_veP"], n_wingg_g)
                    if nn >= 2:
                        sync.wait_ge(S["s_veW"], nn - 1)
                    srcw = WdnT.rearrange("(a p) d -> p a d", p=P)
                    srcw = srcw[b, :, dsl * SW:(dsl + 1) * SW]
                    sync.dma_start(out=prepf[0:64, slot * SW:(slot + 1) * SW],
                                   in_=srcw).then_inc(
                        S["s_plBE" if nn % 2 == 0 else "s_plBO"], 16)
                for nn in range(2):
                    fold_load(nn)
                nldB = 2
                for fa in range(NFA):
                    b, dsl = fa // 4, fa % 4
                    sync.wait_ge(S["s_seFa"], fa + 1)
                    sync.dma_start(out=foldV[:, b, dsl * SW:(dsl + 1) * SW],
                                   in_=folde[0:64, (fa % 2) * SW:(fa % 2 + 1) * SW]
                                   ).then_inc(S["s_folds"], 16)
                    if nldB < NFA:
                        fold_load(nldB)
                        nldB += 1

            def fold_stepB_dmas():
                def bload(a_):
                    for sub in range(2):
                        slot = sub
                        if a_ >= 1:
                            sync.wait_ge(S["s_mmfB"], a_)
                        sync.wait_ge(S["s_folds"], 16 * NFA)
                        sync.dma_start(out=foldst[:, slot * 1024:(slot + 1) * 1024]
                                       .rearrange("p (n q) -> p n q", n=2),
                                       in_=foldV[a_, :, sub * 1024:(sub + 1) * 1024]
                                       .rearrange("p (n q) -> p n q", n=2)
                                       ).then_inc(S["s_foldld"], 16)
                bload(0)
                for a_ in range(NFB):
                    d0 = blkWdn[(0 * NU + a_) * 8:(0 * NU + a_) * 8 + 8].rearrange("n p q -> p n q")
                    d1 = blkWdn[(1 * NU + a_) * 8:(1 * NU + a_) * 8 + 8].rearrange("n p q -> p n q")
                    sync.wait_ge(S["s_seFe"], 2 * a_ + 1)
                    sync.dma_start(out=d0, in_=folde[:, 0:1024].rearrange(
                        "p (n q) -> p n q", n=8)).then_inc(S["s_fbA"], 16)
                    sync.wait_ge(S["s_seFe"], 2 * a_ + 2)
                    sync.dma_start(out=d1, in_=folde[:, 0:1024].rearrange(
                        "p (n q) -> p n q", n=8)).then_inc(S["s_fbB"], 16)
                    if a_ + 1 < NFB:
                        bload(a_ + 1)

            for ei, e in enumerate(wplan):
                if do_fold and e["tag"] == ("wg", 0, 2):
                    fold_loads_storesA()
                if do_fold and e["tag"] == ("wdn", 0, 0, 0):
                    fold_stepB_dmas()
                if e["gate"] is not None:
                    gname, gval = e["gate"]
                    if gname in ("s_prep", "s_prep2"):
                        for sn, sv in par_all(gname, gval):
                            sync.wait_ge(S[sn], sv)
                    elif gname == "s_fb":
                        sync.wait_ge(S["s_fbA"], gval)
                        sync.wait_ge(S["s_fbB"], gval)
                    else:
                        sync.wait_ge(S[gname], gval)
                if e["prev"] is not None:
                    sname, sval = consumed_sem(e["prev"])
                    sync.wait_ge(S[sname], sval)
                src = e["src"][e["blk0"]:e["blk0"] + e["nblk"]]
                dst = stag_slot(e["slot"])[:, 0:e["nblk"] * P]
                sync.dma_start(out=dst.rearrange("p (n q) -> p n q", n=e["nblk"]),
                               in_=src.rearrange("n p q -> p n q")).then_inc(
                    S[f"s_w{e['slot']}"], 16)

        @block.scalar
        def _(sc):
            for srcap, dstap in aux_srcs:
                sc.dma_start(out=dstap, in_=srcap).then_inc(S["s_aux"], 16)
            sc.wait_ge(S["s_aux"], AUXV)
            sc.activation(out=colA[:], in_=alograw, func=AF.Exp).then_inc(S["s_seAx"], 1)
            sc.wait_ge(S["s_veAux"], 1)

            sc.dma_start(out=bigA.rearrange("p (c t) -> p c t", c=ND),
                         in_=xT.rearrange("(c p) t -> p c t", p=P)[:, :, 0:T]
                         ).then_inc(S["s_xin"], 16)

            def prep_load(n):
                fam, g, sub = prep_list[n]
                src, _ = prep_src_dst(fam, g, sub)
                slot = n % 2
                if n >= 2:
                    sc.wait_ge(S["s_veP"], n - 1)
                sc.dma_start(out=prepf[:, slot * SW:(slot + 1) * SW].rearrange(
                    "p (n q) -> p n q", n=SW // P), in_=src).then_inc(
                    S["s_plE" if n % 2 == 0 else "s_plO"], 16)

            def prep_store(n):
                fam, g, sub = prep_list[n]
                _, dst = prep_src_dst(fam, g, sub)
                slot = n % 2
                sc.wait_ge(S["s_veP"], n + 1)
                sc.dma_start(out=dst, in_=prepb[:, slot * SW:(slot + 1) * SW].rearrange(
                    "p (n q) -> p n q", n=SW // P)).then_inc(
                    S["s_prepE" if n % 2 == 0 else "s_prepO"], 16)

            n_wingg = (fam_slabs["win"] + fam_slabs["wg"]) if "wg" in fams else fam_slabs["win"]
            n_wingg = min(n_wingg, NPREP)
            nld = 0
            for n in range(min(2, n_wingg)):
                prep_load(n)
                nld += 1
            for n in range(n_wingg):
                prep_store(n)
                if nld < n_wingg:
                    prep_load(nld)
                    nld += 1

            def phaseA_scalar(j):
                for c in range(NI):
                    g = NI * j + c
                    sc.wait_ge(S["s_mmA"], g + 1)
                    if g >= 2:
                        sc.wait_ge(S["s_veCv"], g - 1)
                    sc.copy(out=projbuf[:, (c % 2) * (T + 3) + 3:(c % 2) * (T + 3) + 3 + T],
                            in_=pb[c % 3][:]).then_inc(S["s_seE"], 1)
                for c in range(NI):
                    g = NI * j + c
                    sc.wait_ge(S["s_veCv"], g + 1)
                    if j > 0 and c == 0 and NTAP:
                        sc.wait_ge(S["s_veTap"], 16 * NTAP * j)
                    sc.activation(out=ch(X, c), in_=ch(X, c),
                                  func=AF.Silu, bias=cbcol[:, c:c + 1]).then_inc(S["s_seX"], 1)
                if j > 0 and have(5):
                    sc.wait_ge(S["s_vePr"], NU * j)
                sc.wait_ge(S["s_mmDtc"], NI * (j + 1))
                sc.activation(out=gbuf[0:64, 0:T], in_=pb[4][0:64, :], func=AF.Abs,
                              bias=dtbs[:])
                sc.activation(out=gbuf[0:64, 0:T], in_=gbuf[0:64, 0:T], func=AF.Exp,
                              scale=-1.0)
                sc.activation(out=gbuf[0:64, 0:T], in_=gbuf[0:64, 0:T], func=AF.Ln,
                              bias=c_one[0:64])
                sc.activation(out=gbuf[0:64, T:2 * T], in_=pb[4][0:64, :], func=AF.Relu,
                              bias=dtbs[:]).then_inc(S["s_seSp"], 1)
                sc.wait_ge(S["s_veDt"], j + 1)
                sc.dma_start(out=dt128[64:128, :], in_=dt128[0:64, :]).then_inc(S["s_dup"], 16)
                sc.wait_ge(S["s_veAux"], 1)
                sc.wait_ge(S["s_dup"], 16 * (j + 1))
                for c in range(NI):
                    g = NI * j + c
                    if g >= 1:
                        sc.wait_ge(S["s_veScn"], g)
                    sc.activation(out=sc_dk[:], in_=dt128[:],
                                  func=AF.Exp, scale=colA[:, c:c + 1]).then_inc(S["s_seDk"], 1)

            def stepA_evac_bundle(lo, hi):
                for fa in range(lo, hi):
                    sc.wait_ge(S["s_mmfA"], fa + 1)
                    if fa >= 2:
                        sc.wait_ge(S["s_folds"], 16 * (fa - 1))
                    sc.copy(out=folde[0:64, (fa % 2) * SW:(fa % 2 + 1) * SW],
                            in_=pb[4 + fa % 4][0:64, :]).then_inc(S["s_seFa"], 1)

            def gate_evacs(j):
                for c in range(NI):
                    g = NI * j + c
                    sc.wait_ge(S["s_mmG"], g + 1)
                    if g >= 2:
                        sc.wait_ge(S["s_veCb"], g - 1)
                    sc.activation(out=gbuf[:, (c % 2) * T:(c % 2 + 1) * T],
                                  in_=pb[c % 3][:], func=AF.Sigmoid).then_inc(S["s_seG"], 1)

            def ln_scalar(j, ln, statsem, statval):
                for c in range(ND):
                    idx = NSTT * j + ln * ND + c
                    sc.wait_ge(S[statsem], statval(c))
                    if idx >= 1:
                        sc.wait_ge(S["s_mmSc"], idx)
                    sc.activation(out=statq[:],
                                  in_=ch(bigA, c), func=AF.Square).then_inc(S["s_seQ"], 1)
                sc.wait_ge(S["s_veV"], NLN * j + ln + 1)
                sc.activation(out=r_var, in_=r_var, func=AF.Sqrt,
                              bias=c_eps[0:1]).then_inc(S["s_seSqt"], 1)

            def fwht_evacs(j):
                for a in range(ND):
                    sc.wait_ge(S["s_mmF"], ND * j + a + 1)
                    sc.copy(out=ch(regB, a), in_=pb[a % 4][:]).then_inc(S["s_seF"], 1)

            def gf_silus(j):
                for u in range(NU):
                    g = NU * j + u
                    sc.wait_ge(S["s_mmGF"], g + 1)
                    if g >= 2:
                        sc.wait_ge(S["s_vePr"], g - 1)
                    sc.activation(out=gbuf[:, (u % 2) * T:(u % 2 + 1) * T],
                                  in_=pb[(u % 2) * 2 + 1][:], func=AF.Silu,
                                  scale=colwg[:, u:u + 1]).then_inc(S["s_seGF"], 1)

            def stepB_evacs():
                for a_ in range(NFB):
                    sc.wait_ge(S["s_mmfB"], a_ + 1)
                    if a_ >= 1:
                        sc.wait_ge(S["s_foldblk"], 32 * a_)
                    sc.activation(out=folde[:, 0:2048],
                                  in_=nc_psum_row4(), func=AF.Copy,
                                  scale=c_rsq8).then_inc(S["s_seFe"], 1)

            def nc_psum_row4():
                raise RuntimeError("placeholder")

            for j in range(NT):
                phaseA_scalar(j)
                if j == 0 and do_fold:
                    stepA_evac_bundle(0, NFA)
                if have(4):
                    gate_evacs(j)
                    ln_scalar(j, 0, "s_veR", lambda c, jj=j: 16 * jj + c + 1)
                if have(5):
                    fwht_evacs(j)
                    gf_silus(j)
                if have(6):
                    if j == 0:
                        for a_ in range(NFB):
                            sc.wait_ge(S["s_mmfB"], a_ + 1)
                            if a_ >= 1:
                                sc.wait_ge(S["s_fbA"], 16 * a_)
                                sc.wait_ge(S["s_fbB"], 16 * a_)
                            sc.activation(out=folde[:, 0:SW], in_=pb[4][:],
                                          func=AF.Copy, scale=c_rsq8)
                            sc.activation(out=folde[:, SW:2 * SW], in_=pb[5][:],
                                          func=AF.Copy,
                                          scale=c_rsq8).then_inc(S["s_seFe"], 1)
                            sc.wait_ge(S["s_fbA"], 16 * (a_ + 1))
                            sc.activation(out=folde[:, 0:SW], in_=pb[6][:],
                                          func=AF.Copy, scale=c_rsq8)
                            sc.activation(out=folde[:, SW:2 * SW], in_=pb[7][:],
                                          func=AF.Copy,
                                          scale=c_rsq8).then_inc(S["s_seFe"], 1)
                    ln_scalar(j, 1, "s_veR2", lambda c, jj=j: 16 * jj + c + 1)


        @block.gpsimd
        def _(g):
            n_wingg = (fam_slabs["win"] + fam_slabs["wg"]) if "wg" in fams else fam_slabs["win"]
            n_wingg = min(n_wingg, NPREP)

            def prep_load(n):
                fam, gg, sub = prep_list[n]
                src, _ = prep_src_dst(fam, gg, sub)
                slot = n % 2
                if n >= 2:
                    g.wait_ge(S["s_veP"], n - 1)
                g.dma_start(out=prepf[:, slot * SW:(slot + 1) * SW].rearrange(
                    "p (n q) -> p n q", n=SW // P), in_=src).then_inc(
                    S["s_pl2E" if n % 2 == 0 else "s_pl2O"], 16)

            def prep_store(n):
                fam, gg, sub = prep_list[n]
                _, dst = prep_src_dst(fam, gg, sub)
                slot = n % 2
                g.wait_ge(S["s_veP"], n + 1)
                g.dma_start(out=dst, in_=prepb[:, slot * SW:(slot + 1) * SW].rearrange(
                    "p (n q) -> p n q", n=SW // P)).then_inc(
                    S["s_prep2E" if n % 2 == 0 else "s_prep2O"], 16)

            nld = n_wingg
            for n in range(min(n_wingg + 2, NPREP) - n_wingg):
                if do_fold and n == 0:
                    g.wait_ge(S["s_veW"], NFA)
                prep_load(n_wingg + n)
                nld = n_wingg + n + 1
            for n in range(n_wingg, NPREP):
                if n == n_wingg:
                    for sn, sv in par_all("s_prep", n_wingg):
                        g.wait_ge(S[sn], sv)
                prep_store(n)
                if nld < NPREP:
                    prep_load(nld)
                    nld += 1

            for j in range(NT):
                if j > 0:
                    if NTAP:
                        g.wait_ge(S["s_veTap"], 16 * NTAP * j)
                    if have(6):
                        g.wait_ge(S["s_veO"], ND * j)
                    elif have(4):
                        g.wait_ge(S["s_veN"], ND * j)
                    else:
                        g.wait_ge(S["s_veXc"], ND * j)
                    g.dma_start(out=bigA.rearrange("p (c t) -> p c t", c=ND),
                                in_=xT.rearrange("(c p) t -> p c t", p=P)[:, :, j * T:(j + 1) * T]
                                ).then_inc(S["s_xin2"], 16)
                if "X" in tapd:
                    g.wait_ge(S["s_seX"], NI * (j + 1))
                    g.dma_start(out=tapd["X"].rearrange("(c p) t -> p c t", p=P)
                                [:, :, j * T:(j + 1) * T],
                                in_=X.rearrange("p (c t) -> p c t", c=NI)).then_inc(S["s_veTap"], 16)
                if "dt128" in tapd:
                    g.wait_ge(S["s_dup"], 16 * (j + 1))
                    g.dma_start(out=tapd["dt128"][:, j * T:(j + 1) * T],
                                in_=dt128[:]).then_inc(S["s_veTap"], 16)
                if "y" in tapd:
                    g.wait_ge(S["s_veCb"], NI * (j + 1))
                    g.dma_start(out=tapd["y"].rearrange("(c p) t -> p c t", p=P)
                                [:, :, j * T:(j + 1) * T],
                                in_=X.rearrange("p (c t) -> p c t", c=NI)).then_inc(S["s_veTap"], 16)
                if "x2n" in tapd:
                    g.wait_ge(S["s_veN"], ND * (j + 1))
                    g.dma_start(out=tapd["x2n"].rearrange("(c p) t -> p c t", p=P)
                                [:, :, j * T:(j + 1) * T],
                                in_=bigA.rearrange("p (c t) -> p c t", c=ND)).then_inc(S["s_veTap"], 16)
                if "prodt" in tapd:
                    g.wait_ge(S["s_vePr"], NU * (j + 1))
                    g.dma_start(out=tapd["prodt"].rearrange("(c p) t -> p c t", p=P)
                                [:, :, j * T:(j + 1) * T],
                                in_=prod.rearrange("p (c t) -> p c t", c=NU)).then_inc(S["s_veTap"], 16)
                if "xht" in tapd:
                    g.wait_ge(S["s_veXh"], ND * (j + 1))
                    g.dma_start(out=tapd["xht"].rearrange("(c p) t -> p c t", p=P)
                                [:, :, j * T:(j + 1) * T],
                                in_=xh.rearrange("p (c t) -> p c t", c=ND)).then_inc(S["s_veTap"], 16)
                if have(6):
                    g.wait_ge(S["s_veO"], ND * (j + 1))
                    g.dma_start(out=outT.rearrange("(c p) t -> p c t", p=P)
                                [:, :, j * T:(j + 1) * T],
                                in_=outr.rearrange("p (c t) -> p c t", c=ND)).then_inc(S["s_out"], 16)
            if have(6):
                g.wait_ge(S["s_out"], 16 * NT)
            if NTAP:
                g.wait_ge(S["s_veTap"], 16 * NTAP * NT)

        @block.tensor
        def _(te):
            for j in range(NT):
                if j == 0:
                    te.wait_ge(S["s_aux"], AUXV)
                if NTAP and j > 0:
                    te.wait_ge(S["s_veTap"], 16 * NTAP * j)
                for c in range(NI):
                    g = NI * j + c
                    te.wait_ge(S[w_wait[("win", j, c)][0]], w_wait[("win", j, c)][1])
                    if c == 0:
                        te.wait_ge(S["s_veXc"], ND * (j + 1))
                        if j > 0:
                            if have(6):
                                te.wait_ge(S["s_veR2"], 16 * j)
                            elif have(5):
                                te.wait_ge(S["s_vePr"], NU * j)
                                te.wait_ge(S["s_seGF"], NU * j)
                            elif have(4):
                                te.wait_ge(S["s_veR"], 16 * j)
                    if g >= 3:
                        te.wait_ge(S["s_seE"], g - 2)
                    slot = w_slotof[("win", j, c)]
                    for k in range(ND):
                        mm = te.matmul(pb[c % 3][:], stag_slot(slot, k), ch(xbf, k),
                                       start=(k == 0), stop=(k == ND - 1))
                    mm.then_inc(S["s_mmA"], 1)

                for c in range(NI):
                    te.wait_ge(S["s_veXb"], NI * j + c + 1)
                    if c == 0:
                        te.wait_ge(S["s_veAux"], 1)
                        if j > 0:
                            te.wait_ge(S["s_seSp"], j)
                        if j == 1 and do_fold:
                            te.wait_ge(S["s_seFe"], 2 * NFB)
                    te.wait_ge(S["s_veLd"], NI * j + c + 1)
                    mm = te.matmul(pb[4][0:64, :], ldiag[:, (c % 2) * 64:(c % 2 + 1) * 64],
                                   xbstag[:, (c % 2) * T:(c % 2 + 1) * T],
                                   start=(c == 0), stop=(c == NI - 1))
                    mm.then_inc(S["s_mmDtc"], 1)

                if j == 0 and do_fold:
                    for fa in range(NFA):
                        te.wait_ge(S["s_veW"], fa + 1)
                        if fa >= 4:
                            te.wait_ge(S["s_seFa"], fa - 3)
                        te.matmul(pb[4 + fa % 4][0:64, :], h64s[:],
                                  prepb[0:64, (fa % 2) * SW:(fa % 2 + 1) * SW],
                                  start=True, stop=True).then_inc(S["s_mmfA"], 1)

                if not have(4):
                    continue
                for c in range(NI):
                    g = NI * j + c
                    te.wait_ge(S[w_wait[("wg", j, c)][0]], w_wait[("wg", j, c)][1])
                    if c == 0:
                        te.wait_ge(S["s_seE"], NI * (j + 1))
                    if g >= 3:
                        te.wait_ge(S["s_seG"], g - 2)
                    slot = w_slotof[("wg", j, c)]
                    for k in range(ND):
                        mm = te.matmul(pb[c % 3][:], stag_slot(slot, k), ch(xbf, k),
                                       start=(k == 0), stop=(k == ND - 1))
                    mm.then_inc(S["s_mmG"], 1)

                te.wait_ge(S["s_seG"], NI * (j + 1))
                te.wait_ge(S["s_seSp"], j + 1)
                if j > 0:
                    te.wait_ge(S["s_veO"] if have(6) else S["s_veN"], ND * j)
                for h in range(2):
                    if h == 1:
                        te.wait_ge(S["s_veR"], 16 * j + 8)
                    for c in range(NI):
                        te.wait_ge(S["s_veCb"], NI * j + c + 1)
                        _wn, _wv = w_wait[("wo", j, h, c // 2)]
                        te.wait_ge(S[_wn], _wv)
                        slot = w_slotof[("wo", j, h, c // 2)]
                        for d in range(8):
                            mm = te.matmul(pb[d][:], stag_slot(slot, (c % 2) * 8 + d),
                                           ch(comb, c), start=(c == 0), stop=(c == NI - 1))
                            if c == NI - 1:
                                mm.then_inc(S["s_mmO"], 1)
                        if c % 2 == 1 and c < NI - 1:
                            mm.then_inc(S["s_mmOc"], 2)

                for c in range(ND):
                    idx = NSTT * j + c
                    te.wait_ge(S["s_veRb"], idx + 1)
                    te.wait_ge(S["s_seQ"], idx + 1)
                    if c == 0:
                        te.wait_ge(S["s_veR"], 16 * (j + 1))
                    mm = te.matmul(pb[5][0:1, :], onesbs[:], statv[:],
                                   start=(c == 0), stop=(c == ND - 1))
                    mm2 = te.matmul(pb[6][0:1, :], onesbs[:], statq[:],
                                    start=(c == 0), stop=(c == ND - 1))
                    mm2.then_inc(S["s_mmSc"], 1)
                te.wait_ge(S["s_veLN"], NLN * j + 1)
                te.matmul(pb[5][:], onesrs[:], r_mean, start=True, stop=True)
                te.matmul(pb[6][:], onesrs[:], r_rstd, start=True,
                          stop=True).then_inc(S["s_mmS"], 1)

                if not have(5):
                    continue
                for a in range(ND):
                    fidx = ND * j + a
                    te.wait_ge(S["s_veXs"], fidx + 1)
                    if fidx >= 4:
                        te.wait_ge(S["s_seF"], fidx - 3)
                    te.matmul(pb[a % 4][:], h128s[:], ch(xbf, a),
                              start=True, stop=True).then_inc(S["s_mmF"], 1)

                for u in range(NU):
                    g = NU * j + u
                    _wn, _wv = w_wait[("wup", j, u)]
                    te.wait_ge(S[_wn], _wv)
                    _wn, _wv = w_wait[("wgf", j, u)]
                    te.wait_ge(S[_wn], _wv)
                    if u == 0:
                        te.wait_ge(S["s_veXh"], ND * (j + 1))
                        te.wait_ge(S["s_seF"], ND * (j + 1))
                    if g >= 2:
                        te.wait_ge(S["s_vePr"], g - 1)
                        te.wait_ge(S["s_seGF"], g - 1)
                    slu = w_slotof[("wup", j, u)]
                    slg = w_slotof[("wgf", j, u)]
                    bu, bg = (u % 2) * 2, (u % 2) * 2 + 1
                    for k in range(ND):
                        mm = te.matmul(pb[bu][:], stag_slot(slu, k), ch(xh, k),
                                       start=(k == 0), stop=(k == ND - 1))
                    mm.then_inc(S["s_mmU"], 1)
                    for k in range(ND):
                        mm = te.matmul(pb[bg][:], stag_slot(slg, k), ch(xh, k),
                                       start=(k == 0), stop=(k == ND - 1))
                    mm.then_inc(S["s_mmGF"], 1)

                if not have(6):
                    continue
                if j == 0:
                    te.wait_ge(S["s_veN"], ND)
                    te.wait_ge(S["s_seSp"], 1)
                    for a_ in range(NFB):
                        te.wait_ge(S["s_foldld"], 16 * 2 * (a_ + 1))
                        if a_ >= 1:
                            te.wait_ge(S["s_seFe"], 2 * a_)
                        for q in range(4):
                            src = foldst[:, (q // 2) * 1024 + (q % 2) * SW:
                                         (q // 2) * 1024 + (q % 2 + 1) * SW]
                            mm = te.matmul(pb[4 + q][:], h128s[:], src,
                                           start=True, stop=True)
                        mm.then_inc(S["s_mmfB"], 1)

                te.wait_ge(S["s_vePr"], NU * (j + 1))
                te.wait_ge(S["s_seGF"], NU * (j + 1))
                te.wait_ge(S["s_veN"], ND * (j + 1))
                if j == 0:
                    te.wait_ge(S["s_seFe"], 2 * NFB)
                for h in range(2):
                    if h == 1:
                        te.wait_ge(S["s_veR2"], 16 * j + 8)
                    for a in range(NU):
                        _wn, _wv = w_wait[("wdn", j, h, a // 2)]
                        te.wait_ge(S[_wn], _wv)
                        slot = w_slotof[("wdn", j, h, a // 2)]
                        for m in range(8):
                            mm = te.matmul(pb[m][:], stag_slot(slot, (a % 2) * 8 + m),
                                           ch(prod, a), start=(a == 0), stop=(a == NU - 1))
                            if a == NU - 1:
                                mm.then_inc(S["s_mmD"], 1)
                        if a % 2 == 1 and a < NU - 1:
                            mm.then_inc(S["s_mmDa"], 2)

                for c in range(ND):
                    idx = NSTT * j + ND + c
                    te.wait_ge(S["s_veRb"], idx + 1)
                    te.wait_ge(S["s_seQ"], idx + 1)
                    if c == 0:
                        te.wait_ge(S["s_veR2"], 16 * (j + 1))
                    mm = te.matmul(pb[5][0:1, :], onesbs[:], statv[:],
                                   start=(c == 0), stop=(c == ND - 1))
                    mm2 = te.matmul(pb[6][0:1, :], onesbs[:], statq[:],
                                    start=(c == 0), stop=(c == ND - 1))
                    mm2.then_inc(S["s_mmSc"], 1)
                te.wait_ge(S["s_veLN"], NLN * j + 2)
                te.matmul(pb[5][:], onesrs[:], r_mean, start=True, stop=True)
                te.matmul(pb[6][:], onesrs[:], r_rstd, start=True,
                          stop=True).then_inc(S["s_mmS"], 1)

        @block.vector
        def _(ve):
            ve.wait_ge(S["s_aux"], AUXV)
            ve.memset(cols[:, 0:1], 1.0)
            ve.memset(cols[:, 1:2], 1e-5)
            ve.memset(cols[:, 2:3], 1.0 / 2048.0)
            ve.memset(cols[:, 3:4], RSQ2048)
            ve.memset(cols[:, 4:5], RSQ8192)
            ve.memset(margin[:], 0.0)
            ve.memset(carry[:], 0.0).then_inc(S["s_veMs"], 1)
            ve.wait_ge(S["s_veMs"], 1)
            ve.wait_ge(S["s_seAx"], 1)
            ve.tensor_scalar(out=colA[:], in0=colA[:], scalar1=-1.0, scalar2=None,
                             op0=ALU.mult)
            ve.tensor_scalar(out=colwu[:], in0=colwu[:], scalar1=c_rsq2,
                             scalar2=None, op0=ALU.mult)
            ve.tensor_tensor(out=colwu[:], in0=colwu[:], in1=colsu2[:], op=ALU.mult)
            ve.tensor_scalar(out=colwg[:], in0=colwg[:], scalar1=c_rsq2,
                             scalar2=None, op0=ALU.mult).then_inc(S["s_veAux"], 1)
            ve.wait_ge(S["s_veAux"], 1)

            def prep_cast(n):
                slot = n % 2
                if n < n_wingg:
                    sn, sv = par("s_pl", n)
                else:
                    sn, sv = par("s_pl2", n - n_wingg)
                ve.wait_ge(S[sn], sv)
                if n >= 2:
                    nn = n - 2
                    if nn < n_wingg:
                        sn, sv = par("s_prep", nn)
                    else:
                        sn, sv = par("s_prep2", nn - n_wingg)
                    ve.wait_ge(S[sn], sv)
                ve.tensor_scalar(out=prepb[:, slot * SW:(slot + 1) * SW],
                                 in0=prepf[:, slot * SW:(slot + 1) * SW],
                                 scalar1=1.0, scalar2=None,
                                 op0=ALU.mult).then_inc(S["s_veP"], 1)

            def fold_cast(nn):
                slot = nn % 2
                if nn == 0:
                    for sn, sv in par_all("s_prep", n_wingg):
                        ve.wait_ge(S[sn], sv)
                sn, sv = par("s_plB", nn)
                ve.wait_ge(S[sn], sv)
                if nn >= 2:
                    ve.wait_ge(S["s_mmfA"], nn - 1)
                ve.tensor_scalar(out=prepb[0:64, slot * SW:(slot + 1) * SW],
                                 in0=prepf[0:64, slot * SW:(slot + 1) * SW],
                                 scalar1=1.0, scalar2=None,
                                 op0=ALU.mult).then_inc(S["s_veW"], 1)

            n_wingg = (fam_slabs["win"] + fam_slabs["wg"]) if "wg" in fams else fam_slabs["win"]
            n_wingg = min(n_wingg, NPREP)

            for j in range(NT):
                if j == 0:
                    ve.wait_ge(S["s_xin"], 16)
                else:
                    ve.wait_ge(S["s_xin2"], 16 * j)
                for c in range(ND):
                    if j > 0 and c == 0:
                        if have(5):
                            ve.wait_ge(S["s_mmF"], ND * j)
                        elif have(4):
                            ve.wait_ge(S["s_mmG"], NI * j)
                        else:
                            ve.wait_ge(S["s_mmA"], NI * j)
                    ve.tensor_scalar(out=ch(xbf, c), in0=ch(bigA, c), scalar1=1.0,
                                     scalar2=None, op0=ALU.mult).then_inc(S["s_veXc"], 1)

                if j == 0:
                    for n in range(n_wingg):
                        prep_cast(n)

                for c in range(NI):
                    g = NI * j + c
                    pslot = (c % 2) * (T + 3)
                    ve.tensor_copy(out=projbuf[:, pslot:pslot + 3],
                                   in_=margin[:, c * 3:(c + 1) * 3])
                    ve.wait_ge(S["s_seE"], g + 1)
                    acc = ch(X, c)
                    ve.tensor_scalar(out=acc, in0=projbuf[:, pslot:pslot + T],
                                     scalar1=cw[0][:, c:c + 1], scalar2=None, op0=ALU.mult)
                    for kk in range(1, 4):
                        ve.scalar_tensor_tensor(out=acc,
                                                in0=projbuf[:, pslot + kk:pslot + kk + T],
                                                scalar=cw[kk][:, c:c + 1], in1=acc,
                                                op0=ALU.mult, op1=ALU.add)
                    ve.tensor_copy(out=margin[:, c * 3:(c + 1) * 3],
                                   in_=projbuf[:, pslot + T:pslot + T + 3]).then_inc(S["s_veCv"], 1)
                for c in range(NI):
                    g = NI * j + c
                    if g >= 2:
                        ve.wait_ge(S["s_mmDtc"], g - 1)
                    ve.tensor_scalar(out=ldiag[:, (c % 2) * 64:(c % 2 + 1) * 64],
                                     in0=eye2s[:], scalar1=dtwcol[:, c:c + 1],
                                     scalar2=None, op0=ALU.mult).then_inc(S["s_veLd"], 1)
                    ve.wait_ge(S["s_seX"], g + 1)
                    ve.tensor_scalar(out=xbstag[:, (c % 2) * T:(c % 2 + 1) * T],
                                     in0=ch(X, c), scalar1=1.0, scalar2=None,
                                     op0=ALU.mult).then_inc(S["s_veXb"], 1)

                if "dt128" in tapd and j > 0:
                    tpos = sorted(tapd).index("dt128") + 1
                    ve.wait_ge(S["s_veTap"], 16 * ((j - 1) * NTAP + tpos))
                ve.wait_ge(S["s_seSp"], j + 1)
                ve.tensor_tensor(out=dt128[0:64, :], in0=gbuf[0:64, 0:T],
                                 in1=gbuf[0:64, T:2 * T], op=ALU.add).then_inc(S["s_veDt"], 1)

                if "X" in tapd:
                    tpos = sorted(tapd).index("X") + 1
                    ve.wait_ge(S["s_veTap"], 16 * (j * NTAP + tpos))
                ve.wait_ge(S["s_dup"], 16 * (j + 1))
                for c in range(NI):
                    g = NI * j + c
                    ve.scalar_tensor_tensor(out=sc_dr[:], in0=dt128[:],
                                            scalar=bcol[:, c:c + 1], in1=ch(X, c),
                                            op0=ALU.mult, op1=ALU.mult)
                    ve.wait_ge(S["s_seDk"], g + 1)
                    ve.tensor_tensor_scan(out=sc_st[:],
                                          data0=sc_dk[:],
                                          data1=sc_dr[:], initial=carry[:, c:c + 1],
                                          op0=ALU.mult, op1=ALU.add).then_inc(S["s_veScn"], 1)
                    ve.tensor_copy(out=carry[:, c:c + 1], in_=sc_st[:, T - 1:T])
                    ve.tensor_scalar(out=sc_dr[:], in0=ch(X, c),
                                     scalar1=dcol[:, c:c + 1], scalar2=None, op0=ALU.mult)
                    yop = ve.scalar_tensor_tensor(out=ch(X, c), in0=sc_st[:],
                                                  scalar=ccol[:, c:c + 1], in1=sc_dr[:],
                                                  op0=ALU.mult, op1=ALU.add)
                    if not have(4) and c == NI - 1:
                        yop.then_inc(S["s_veCb"], NI)

                if j == 0 and do_fold:
                    for nn in range(NFA):
                        fold_cast(nn)

                if not have(4):
                    continue

                if j == 0 and "wo" in fams:
                    if do_fold:
                        ve.wait_ge(S["s_mmfA"], NFA)
                    for n in range(n_wingg, n_wingg + fam_slabs["wo"]):
                        prep_cast(n)

                for c in range(NI):
                    g = NI * j + c
                    if c == 0 and j > 0 and have(6):
                        ve.wait_ge(S["s_out"], 16 * j)
                    ve.wait_ge(S["s_seG"], g + 1)
                    ve.tensor_tensor(out=ch(comb, c), in0=ch(X, c),
                                     in1=gbuf[:, (c % 2) * T:(c % 2 + 1) * T],
                                     op=ALU.mult).then_inc(S["s_veCb"], 1)

                for hc in range(16):
                    ve.wait_ge(S["s_mmO"], 16 * j + hc + 1)
                    ve.tensor_tensor(out=ch(bigA, hc), in0=pb[hc % 8][:],
                                     in1=ch(bigA, hc), op=ALU.add).then_inc(S["s_veR"], 1)

                def ln_vector(j, ln, src_ap, dst_ap, nsem, scount_base):
                    for c in range(ND):
                        idx = NSTT * j + ln * ND + c
                        if idx >= 1:
                            ve.wait_ge(S["s_mmSc"], idx)
                        ve.tensor_scalar(out=statv[:],
                                         in0=ch(src_ap, c), scalar1=1.0, scalar2=None,
                                         op0=ALU.mult).then_inc(S["s_veRb"], 1)
                    ve.wait_ge(S["s_mmSc"], NSTT * j + ln * ND + ND)
                    ve.tensor_scalar(out=r_mean, in0=pb[5][0:1, :], scalar1=c_i2048[0:1],
                                     scalar2=None, op0=ALU.mult)
                    ve.tensor_scalar(out=r_var, in0=pb[6][0:1, :], scalar1=c_i2048[0:1],
                                     scalar2=None, op0=ALU.mult)
                    ve.tensor_tensor(out=r_rstd, in0=r_mean, in1=r_mean, op=ALU.mult)
                    ve.tensor_tensor(out=r_var, in0=r_var, in1=r_rstd,
                                     op=ALU.subtract).then_inc(S["s_veV"], 1)
                    ve.wait_ge(S["s_seSqt"], NLN * j + ln + 1)
                    ve.reciprocal(out=r_rstd, in_=r_var).then_inc(S["s_veLN"], 1)
                    ve.wait_ge(S["s_mmS"], NLN * j + ln + 1)
                    for c in range(ND):
                        ve.tensor_tensor(out=ch(src_ap, c), in0=ch(src_ap, c),
                                         in1=pb[5][:], op=ALU.subtract)
                        ve.tensor_tensor(out=ch(dst_ap, c), in0=ch(src_ap, c),
                                         in1=pb[6][:], op=ALU.mult).then_inc(S[nsem], 1)

                ln_vector(j, 0, bigA, bigA, "s_veN", None)

                if not have(5):
                    continue
                for c in range(ND):
                    if c == 0:
                        ve.wait_ge(S["s_mmG"], NI * (j + 1))
                    ve.tensor_scalar(out=ch(xbf, c), in0=ch(bigA, c),
                                     scalar1=sucol[:, c:c + 1], scalar2=None,
                                     op0=ALU.mult).then_inc(S["s_veXs"], 1)

                if j == 0 and "wup" in fams:
                    for n in range(n_wingg + fam_slabs["wo"], NPREP):
                        prep_cast(n)

                ve.wait_ge(S["s_seF"], ND * (j + 1))
                regs = [regB, regC]
                for st_ in range(4):
                    bit = 1 << st_
                    src_, dst_ = regs[st_ % 2], regs[(st_ + 1) % 2]
                    for a in range(ND):
                        if a & bit:
                            ve.tensor_tensor(out=ch(dst_, a), in0=ch(src_, a ^ bit),
                                             in1=ch(src_, a), op=ALU.subtract)
                        else:
                            ve.tensor_tensor(out=ch(dst_, a), in0=ch(src_, a),
                                             in1=ch(src_, a ^ bit), op=ALU.add)
                for a in range(ND):
                    ve.tensor_scalar(out=ch(xh, a), in0=ch(regB, a), scalar1=1.0,
                                     scalar2=None, op0=ALU.mult).then_inc(S["s_veXh"], 1)

                for u in range(NU):
                    g = NU * j + u
                    ve.wait_ge(S["s_mmU"], g + 1)
                    ve.wait_ge(S["s_seGF"], g + 1)
                    ve.scalar_tensor_tensor(out=ch(prod, u), in0=pb[(u % 2) * 2][:],
                                            scalar=colwu[:, u:u + 1],
                                            in1=gbuf[:, (u % 2) * T:(u % 2 + 1) * T],
                                            op0=ALU.mult, op1=ALU.mult).then_inc(S["s_vePr"], 1)

                if not have(6):
                    continue
                for hc in range(16):
                    ve.wait_ge(S["s_mmD"], 16 * j + hc + 1)
                    ve.scalar_tensor_tensor(out=ch(bigA, hc), in0=pb[hc % 8][:],
                                            scalar=wsdcol[:, hc:hc + 1], in1=ch(bigA, hc),
                                            op0=ALU.mult, op1=ALU.add).then_inc(S["s_veR2"], 1)
                ln_vector(j, 1, bigA, outr, "s_veO", None)

    return nc



def host_inputs(inputs, taps=()):
    x = np.asarray(inputs["L0_attn_hidden_states"])
    perm = (np.arange(INNER) % 64) * 64 + (np.arange(INNER) // 64)
    win = np.ascontiguousarray(np.asarray(inputs["L0_attn_in_proj_w"])[perm].T)
    wg = np.ascontiguousarray(np.asarray(inputs["L0_attn_gate_proj_w"])[perm].T)
    wo = np.ascontiguousarray(np.asarray(inputs["L0_attn_out_proj_w"])[:, perm].T)
    wup = np.ascontiguousarray(np.asarray(inputs["L0_ffn_W_up"]).T)
    wgf = np.ascontiguousarray(np.asarray(inputs["L0_ffn_W_gate"]).T)
    wdn = np.ascontiguousarray(np.asarray(inputs["L0_ffn_W_down"]).T)

    def colbank(v, nchunk):
        return np.ascontiguousarray(np.asarray(v).reshape(nchunk, P).T)

    convw = np.asarray(inputs["L0_attn_conv_w"])[perm]
    convw_c = np.ascontiguousarray(
        np.concatenate([colbank(convw[:, k], NI) for k in range(4)], axis=1))
    convb_c = colbank(np.asarray(inputs["L0_attn_conv_b"])[perm], NI)
    tp = lambda M: np.asarray(M).T.reshape(-1)
    alog_c = colbank(tp(inputs["L0_attn_A_log"]), NI)
    b_c = colbank(tp(inputs["L0_attn_Bp"]), NI)
    c_c = colbank(tp(inputs["L0_attn_Cp"]), NI)
    d_c = colbank(tp(inputs["L0_attn_Dp"]), NI)
    dtw_c = colbank(tp(inputs["L0_attn_dt_w"]), NI)
    dtb_c = np.ascontiguousarray(np.asarray(inputs["L0_attn_dt_b"]).reshape(64, 1))
    eye2 = np.ascontiguousarray(np.concatenate([np.eye(64, dtype=np.float32)] * 2, 0))
    h128 = _had(128).astype(ml_dtypes.bfloat16)
    h64 = _had(64).astype(ml_dtypes.bfloat16)
    suug_c = colbank(inputs["L0_ffn_SU_ug"], ND)
    ws = np.asarray(inputs["L0_ffn_Wscale_ug"])
    wsu_c = colbank(ws[:INTER], NU)
    wsg_c = colbank(ws[INTER:], NU)
    sudp_c = colbank(inputs["L0_ffn_SU_dp"], NU)
    wsd_c = colbank(inputs["L0_ffn_Wscale_dp"], ND)
    onesb = np.ones((P, 1), ml_dtypes.bfloat16)
    onesrow = np.ones((1, P), np.float32)

    shared = dict(WinT=win, WgT=wg, WoT=wo, WupT=wup, WgfT=wgf, WdnT=wdn,
                  convw_c=convw_c, convb_c=convb_c, alog_c=alog_c, b_c=b_c,
                  c_c=c_c, d_c=d_c, dtw_c=dtw_c, dtb_c=dtb_c, eye2_d=eye2,
                  h128_d=h128, h64_d=h64, suug_c=suug_c, wsu_c=wsu_c,
                  wsg_c=wsg_c, sudp_c=sudp_c, wsd_c=wsd_c, onesb_d=onesb,
                  onesrow_d=onesrow)
    shared = {k: np.ascontiguousarray(np.asarray(v, dtype=(np.float32 if np.asarray(v).dtype == np.float64 else np.asarray(v).dtype))) for k, v in shared.items()}
    in_maps = []
    for core in range(8):
        m = dict(shared)
        m["xT"] = np.ascontiguousarray(np.asarray(x[core]).T)
        in_maps.append(m)
    return in_maps


def kernel(**inputs):
    in_maps = host_inputs(inputs)
    nc = build_program(stage=6)
    res = run_bass_kernel_spmd(nc, in_maps, core_ids=list(range(8)))
    out = np.stack([np.ascontiguousarray(np.asarray(r["outT"]).T)
                    for r in res.results], axis=0)
    return out
